# version 24
# baseline (speedup 1.0000x reference)
from contextlib import ExitStack

import numpy as np
import ml_dtypes

import concourse.bass as bass
import concourse.mybir as mybir
from concourse.bass_utils import run_bass_kernel_spmd

F32 = mybir.dt.float32
BF16 = mybir.dt.bfloat16
AF = mybir.ActivationFunctionType
OP = mybir.AluOpType

D = 2048
T = 2048
H = 16
DH = 128
HID = 8192
SW = 512
EXP_SHIFT = 4.0
EPS = 1e-5


class Sem:
    def __init__(self, nc, es, name):
        self.h = es.enter_context(nc.semaphore(name))
        self.n = 0

    def inc(self, inst, k=1):
        inst.then_inc(self.h, k)
        self.n += k
        return self.n


class DmaQ:

    def __init__(self, nc, es, name, eng, k=4):
        self.eng = eng
        self.sems = [Sem(nc, es, f"{name}{i}") for i in range(k)]
        self.k = k
        self.i = 0
        self.toks = []

    def dma(self, out, in_):
        s = self.sems[self.i % self.k]
        if s.n > 0:
            self.eng.wait_ge(s.h, s.n)
        inst = self.eng.dma_start(out, in_)
        s.inc(inst, 16)
        tok = (s, s.n)
        self.toks.append(tok)
        self.i += 1
        return tok

    def wait_all(self, eng):
        for s in self.sems:
            if s.n > 0:
                eng.wait_ge(s.h, s.n)

    @staticmethod
    def wait_toks(eng, toks):
        best = {}
        for s, v in toks:
            if v > best.get(id(s), (None, -1))[1]:
                best[id(s)] = (s, v)
        for s, v in best.values():
            eng.wait_ge(s.h, v)


def wtok(eng, tok):
    eng.wait_ge(tok[0].h, tok[1])


def build(nc, d=D, t=T, h=H, hid=HID):
    KC = d // 128
    TC = t // 128
    S = t // SW
    HC = hid // 128
    TH = t // 2
    FSLAB = 8
    NSLAB = HC // FSLAB
    WDR = 2 * FSLAB
    assert h * DH == d and t % SW == 0 and TH % SW == 0

    xT = nc.dram_tensor("xT", [d, t], F32, kind="ExternalInput")
    Wq = nc.dram_tensor("Wq", [d, d], F32, kind="ExternalInput")
    Wkv = nc.dram_tensor("Wkv", [d, 2 * DH], F32, kind="ExternalInput")
    Wout = nc.dram_tensor("Wout", [d, d], F32, kind="ExternalInput")
    WgT = nc.dram_tensor("WgT", [d, hid], F32, kind="ExternalInput")
    WuT = nc.dram_tensor("WuT", [d, hid], F32, kind="ExternalInput")
    WdT = nc.dram_tensor("WdT", [hid, d], F32, kind="ExternalInput")
    ropeQc = nc.dram_tensor("ropeQc", [128, t], F32, kind="ExternalInput")
    ropeQs = nc.dram_tensor("ropeQs", [128, t], F32, kind="ExternalInput")
    ropeKc = nc.dram_tensor("ropeKc", [128, t], F32, kind="ExternalInput")
    ropeKs = nc.dram_tensor("ropeKs", [128, t], F32, kind="ExternalInput")
    masks = nc.dram_tensor("masks", [128, 4 * SW], BF16, kind="ExternalInput")
    onesb = nc.dram_tensor("onesb", [128, 1], BF16, kind="ExternalInput")
    onesf = nc.dram_tensor("onesf", [1, 128], F32, kind="ExternalInput")
    onesb1 = nc.dram_tensor("onesb1", [1, 128], BF16, kind="ExternalInput")
    cst = nc.dram_tensor("cst", [128, 8], F32, kind="ExternalInput")
    x2_scr = nc.dram_tensor("x2_scr", [d, t], F32)
    g_scr = nc.dram_tensor("g_scr", [hid, t], BF16)
    outT = nc.dram_tensor("outT", [d, t], F32, kind="ExternalOutput")

    es = ExitStack()
    with es:
        qW = DmaQ(nc, es, "qW", nc.gpsimd, 8)
        qL = DmaQ(nc, es, "qL", nc.sync, 8)
        qS = DmaQ(nc, es, "qS", nc.scalar, 8)
        cPE = Sem(nc, es, "cPE")
        cDV = Sem(nc, es, "cDV")
        cSC = Sem(nc, es, "cSC")
        cGP = Sem(nc, es, "cGP")

        ar_consts = nc.auto_arena()
        c_ones = ar_consts.sbuf_tensor("c_ones", [128, 1], BF16)
        c_onesf = ar_consts.sbuf_tensor("c_onesf", [1, 128], F32)
        c_onesb1 = ar_consts.sbuf_tensor("c_onesb1", [1, 128], BF16)
        c_cst = ar_consts.sbuf_tensor("c_cst", [128, 8], F32)
        c_mask = ar_consts.sbuf_tensor("c_mask", [128, 4 * SW], BF16)
        ar_consts.allocate(side="left")

        ar_b = nc.auto_arena()
        wqb = [ar_b.sbuf_tensor(f"wqb{i}", [128, d], BF16) for i in range(2)]
        wkvb = ar_b.sbuf_tensor("wkvb", [128, KC * 2 * DH], BF16)
        rq = [ar_b.sbuf_tensor(f"rq{i}", [128, SW], BF16) for i in range(2)]
        rt1 = [ar_b.sbuf_tensor(f"rt1{i}", [128, SW], BF16) for i in range(2)]
        rt2 = [ar_b.sbuf_tensor(f"rt2{i}", [128, SW], BF16) for i in range(2)]
        ar_b.allocate(side="left")

        ar_ln1 = nc.auto_arena()
        ln1T = [ar_ln1.sbuf_tensor(f"ln1T{k}", [128, t], BF16) for k in range(KC)]
        ar_ln1.allocate(side="left")

        ar_rope = nc.auto_arena()
        r_qc = ar_rope.sbuf_tensor("r_qc", [128, t], F32)
        r_qs = ar_rope.sbuf_tensor("r_qs", [128, t], F32)
        r_kc = ar_rope.sbuf_tensor("r_kc", [128, t], F32)
        r_ks = ar_rope.sbuf_tensor("r_ks", [128, t], F32)
        ar_rope.allocate(side="left")

        ar_a = nc.auto_arena()
        xA = [ar_a.sbuf_tensor(f"AxA{i}", [128, t], F32) for i in range(5)]
        sqb = [ar_a.sbuf_tensor(f"Asq{i}", [128, t], BF16) for i in range(2)]
        tmpA = ar_a.sbuf_tensor("Atmp", [128, t], BF16)
        meanA = ar_a.sbuf_tensor("Ambc", [128, t], BF16)
        rstdA = ar_a.sbuf_tensor("Arbc", [128, t], BF16)
        rowA = [ar_a.sbuf_tensor(f"ArowF{i}", [1, t], F32) for i in range(3)]
        ar_a.allocate(side="left")
        rwA = ([rowA[0][0:1, :], rowA[1][0:1, :], rowA[2][0:1, :]]
               + [sqb[0][0:1, :], sqb[1][0:1, :]])

        for dst, src in ((c_ones, onesb), (c_onesf, onesf), (c_onesb1, onesb1),
                         (c_cst, cst), (c_mask, masks)):
            qS.dma(dst[:], src[:])
        consts_toks = list(qS.toks)
        rope_toks_q = [qS.dma(r_qc[:], ropeQc[:]), qS.dma(r_qs[:], ropeQs[:])]
        K_1_D = c_cst[0:1, 0:1]
        K_EPS = c_cst[0:1, 1:2]
        K_M05 = c_cst[0:1, 2:3]
        K_M1 = c_cst[0:1, 3:4]
        K_SHIFT_P = c_cst[:, 4:5]

        def ln_finalize_bcast(rows, mean_bc, rstd_bc, p_sx, p_sx2, pe_last, st_es, tag):
            rw, rw2, rw3, rbh_m, rbh_r = rows
            DmaQ.wait_toks(nc.scalar, consts_toks)
            nc.scalar.wait_ge(cPE.h, pe_last)
            sc_fin = None
            for s in range(S):
                sl = slice(s * SW, (s + 1) * SW)
                nc.scalar.activation(rw[0:1, sl], p_sx[s], AF.Copy, scale=K_1_D)
                nc.scalar.activation(rw3[0:1, sl], p_sx2[s], AF.Copy, scale=K_1_D)
                sc_fin = cSC.inc(nc.scalar.activation(rw2[0:1, sl], p_sx[s], AF.Square, scale=K_1_D))
            nc.vector.wait_ge(cSC.h, sc_fin)
            dv_var = None
            for s in range(S):
                sl = slice(s * SW, (s + 1) * SW)
                dv_var = cDV.inc(nc.vector.tensor_sub(rw2[0:1, sl], rw3[0:1, sl], rw2[0:1, sl]))
            nc.scalar.wait_ge(cDV.h, dv_var)
            v_ln = cSC.inc(nc.scalar.activation(rw3[0:1, :], rw2[0:1, :], AF.Ln, bias=K_EPS))
            nc.scalar.wait_ge(cSC.h, v_ln)
            sc_rstd = cSC.inc(nc.scalar.activation(rbh_r[0:1, :], rw3[0:1, :], AF.Exp, scale=K_M05))
            nc.vector.wait_ge(cSC.h, sc_rstd)
            dv_hi = cDV.inc(nc.vector.tensor_copy(rbh_m[0:1, :], rw[0:1, :]))
            st_es.close()
            bc_es = ExitStack()
            p_bc1 = bc_es.enter_context(nc.psum_tensor(f"{tag}pbc1", [128, t], F32))
            p_bc2 = bc_es.enter_context(nc.psum_tensor(f"{tag}pbc2", [128, t], F32))
            nc.tensor.wait_ge(cDV.h, dv_hi)
            nc.tensor.wait_ge(cSC.h, sc_rstd)
            pe_bc = None
            for s in range(S):
                sl = slice(s * SW, (s + 1) * SW)
                nc.tensor.matmul(p_bc1[:, sl], c_onesb1[:], rbh_m[0:1, sl], start=True, stop=True,
                                 skip_group_check=True)
                pe_bc = cPE.inc(nc.tensor.matmul(p_bc2[:, sl], c_onesb1[:], rbh_r[0:1, sl],
                                                 start=True, stop=True, skip_group_check=True))
            nc.vector.wait_ge(cPE.h, pe_bc)
            nc.vector.tensor_copy(mean_bc[:], p_bc1[:])
            dv_bc = cDV.inc(nc.vector.tensor_copy(rstd_bc[:], p_bc2[:]))
            bc_es.close()
            return dv_bc, pe_bc

        stA_es = ExitStack()
        pA_st = [stA_es.enter_context(nc.psum_tensor(f"Apst{s}", [128, SW], F32)) for s in range(S)]
        pA_sx = [pA_st[s][0:1, :] for s in range(S)]
        pA_sx2 = [pA_st[s][32:33, :] for s in range(S)]
        a_dv, a_sc, a_pe = [], [], []
        a_ld = [None] * KC

        def issue_a(m):
            q = qL if (m < 2 or m % 2 == 0) else qW
            if m >= 5:
                q.eng.wait_ge(cDV.h, a_dv[m - 5])
                sc_sem, sc_val = a_sc[m - 5]
                q.eng.wait_ge(sc_sem.h, sc_val)
            a_ld[m] = q.dma(xA[m % 5][:], xT[m * 128:(m + 1) * 128, :])

        for m in range(5):
            issue_a(m)
        for m in range(KC):
            wtok(nc.vector, a_ld[m])
            a_dv.append(cDV.inc(nc.vector.tensor_copy(ln1T[m][:], xA[m % 5][:])))
            if m % 2 == 0:
                wtok(nc.scalar, a_ld[m])
                if m >= 2:
                    nc.scalar.wait_ge(cPE.h, a_pe[m - 2])
                a_sc.append((cSC, cSC.inc(nc.scalar.activation(sqb[m % 2][:], xA[m % 5][:], AF.Square))))
            else:
                nc.vector.wait_ge(cDV.h, a_dv[m])
                if m >= 2:
                    nc.vector.wait_ge(cPE.h, a_pe[m - 2])
                a_sc.append((cDV, cDV.inc(nc.vector.tensor_mul(sqb[m % 2][:], ln1T[m][:], ln1T[m][:]))))
            if m == 0:
                DmaQ.wait_toks(nc.tensor, consts_toks)
            nc.tensor.wait_ge(cDV.h, a_dv[m])
            sc_sem, sc_val = a_sc[m]
            nc.tensor.wait_ge(sc_sem.h, sc_val)
            last = None
            for s in range(S):
                sl = slice(s * SW, (s + 1) * SW)
                nc.tensor.matmul(pA_sx[s], c_ones[:], ln1T[m][:, sl],
                                 start=(m == 0), stop=(m == KC - 1), skip_group_check=True)
                last = nc.tensor.matmul(pA_sx2[s], c_ones[:], sqb[m % 2][:, sl],
                                        start=(m == 0), stop=(m == KC - 1), skip_group_check=True)
            a_pe.append(cPE.inc(last))
            if m + 5 < KC:
                issue_a(m + 5)
        wq_ld = []
        Wq_r = Wq.rearrange("(k p) (m c) -> m p k c", p=128, c=128)
        for i in range(2):
            dstw = wqb[i].ap().rearrange("p (k c) -> p k c", c=128)
            wq_ld.append(qW.dma(dstw, Wq_r[i]))
        rope_toks_k = [qW.dma(r_kc[:], ropeKc[:]), qW.dma(r_ks[:], ropeKs[:])]
        dv_bcA, pe_bcA = ln_finalize_bcast(rwA, meanA, rstdA, pA_sx, pA_sx2, a_pe[-1], stA_es, "A")
        a_ap = []
        for m in range(KC):
            if m == 0:
                nc.vector.wait_ge(cDV.h, dv_bcA)
            if m >= 1:
                nc.vector.wait_ge(cDV.h, a_ap[m - 1])
            v_sub = cDV.inc(nc.vector.tensor_sub(tmpA[:], ln1T[m][:], meanA[:]))
            nc.vector.wait_ge(cDV.h, v_sub)
            a_ap.append(cDV.inc(nc.vector.tensor_mul(ln1T[m][:], tmpA[:], rstdA[:])))
        ln1_done = a_ap[-1]
        ar_a.free()

        ar_qkv = nc.auto_arena()
        qT = [ar_qkv.sbuf_tensor(f"qT{m}", [128, t], BF16) for m in range(h)]
        kTt = ar_qkv.sbuf_tensor("kTt", [128, t], BF16)
        vN = ar_qkv.sbuf_tensor("vN", [128, TC * DH], BF16)
        ar_qkv.allocate(side="right")
        pB_es = ExitStack()
        p_q = [pB_es.enter_context(nc.psum_tensor(f"pq{s}", [128, SW], F32)) for s in range(S)]
        p_v = [pB_es.enter_context(nc.psum_tensor(f"pv{i}", [128, DH], F32)) for i in range(2)]

        Wkv_r = Wkv.rearrange("(k p) c -> p k c", p=128)

        wkv_tok = [None]
        pe_q = []
        sc_rope, dv_rope, gp_rope = [], [], []
        pe_all_rope = []

        def emit_wq_slab(m):
            if m >= 2:
                nc.gpsimd.wait_ge(cPE.h, pe_q[(m - 2) * S + S - 1])
            dst = wqb[m % 2].ap().rearrange("p (k c) -> p k c", c=128)
            wq_ld.append(qW.dma(dst, Wq_r[m]))

        def emit_rope(idx, src_psum, dstT, dst_sl, cosT, sinT):
            nc.scalar.wait_ge(cPE.h, pe_all_rope[idx])
            if idx >= 2:
                nc.scalar.wait_ge(cDV.h, dv_rope[idx - 2])
            sc_rope.append(cSC.inc(nc.scalar.activation(rq[idx % 2][:], src_psum[:], AF.Copy)))
            nc.vector.wait_ge(cSC.h, sc_rope[idx])
            if idx >= 2:
                nc.vector.wait_ge(cDV.h, dv_rope[idx - 2])
            v_t1 = cDV.inc(nc.vector.tensor_mul(rt1[idx % 2][:], rq[idx % 2][:], cosT[:, dst_sl]))
            nc.gpsimd.wait_ge(cSC.h, sc_rope[idx])
            if idx >= 2:
                nc.gpsimd.wait_ge(cDV.h, dv_rope[idx - 2])
            nc.gpsimd.tensor_tensor(rt2[idx % 2][0:64, :], rq[idx % 2][64:128, :],
                                    sinT[64:128, dst_sl], OP.mult)
            gp_rope.append(cGP.inc(nc.gpsimd.tensor_tensor(
                rt2[idx % 2][64:128, :], rq[idx % 2][0:64, :], sinT[0:64, dst_sl], OP.mult)))
            nc.vector.wait_ge(cGP.h, gp_rope[idx])
            nc.vector.wait_ge(cDV.h, v_t1)
            dv_rope.append(cDV.inc(nc.vector.tensor_add(dstT[:, dst_sl], rt1[idx % 2][:], rt2[idx % 2][:])))

        ridx = 0
        for m in range(h):
            wtok(nc.tensor, wq_ld[m])
            if m == 0:
                for k in range(KC):
                    nc.tensor.wait_ge(cDV.h, a_ap[k])
                    for s in range(S):
                        inst = nc.tensor.matmul(p_q[s][:], wqb[0][:, k * 128:(k + 1) * 128],
                                                ln1T[k][:, s * SW:(s + 1) * SW],
                                                start=(k == 0), stop=(k == KC - 1),
                                                skip_group_check=True)
                        if k == KC - 1:
                            pe_q.append(cPE.inc(inst))
            else:
                for s in range(S):
                    nc.tensor.wait_ge(cSC.h, sc_rope[(m - 1) * S + s])
                    inst = None
                    for k in range(KC):
                        inst = nc.tensor.matmul(p_q[s][:], wqb[m % 2][:, k * 128:(k + 1) * 128],
                                                ln1T[k][:, s * SW:(s + 1) * SW],
                                                start=(k == 0), stop=(k == KC - 1), skip_group_check=True)
                    pe_q.append(cPE.inc(inst))
            for s in range(S):
                pe_all_rope.append(pe_q[m * S + s])
                if ridx == 0:
                    DmaQ.wait_toks(nc.vector, rope_toks_q)
                    DmaQ.wait_toks(nc.gpsimd, rope_toks_q)
                emit_rope(ridx, p_q[s], qT[m], slice(s * SW, (s + 1) * SW), r_qc, r_qs)
                ridx += 1
            if m + 2 < h:
                emit_wq_slab(m + 2)
            elif m + 2 == h:
                dst = wkvb.ap().rearrange("p (k c) -> p k c", c=2 * DH)
                wkv_tok[0] = qW.dma(dst, Wkv_r)
        wtok(nc.tensor, wkv_tok[0])
        DmaQ.wait_toks(nc.vector, rope_toks_k)
        DmaQ.wait_toks(nc.gpsimd, rope_toks_k)
        pe_k = []
        for s in range(S):
            nc.tensor.wait_ge(cSC.h, sc_rope[(h - 1) * S + s])
            inst = None
            for k in range(KC):
                inst = nc.tensor.matmul(p_q[s][:], wkvb[:, k * 2 * DH:k * 2 * DH + DH],
                                        ln1T[k][:, s * SW:(s + 1) * SW],
                                        start=(k == 0), stop=(k == KC - 1), skip_group_check=True)
            pe_k.append(cPE.inc(inst))
            pe_all_rope.append(pe_k[s])
            emit_rope(ridx, p_q[s], kTt, slice(s * SW, (s + 1) * SW), r_kc, r_ks)
            ridx += 1
        dv_ropek_last = dv_rope[-1]
        sc_v = []
        for tc in range(TC):
            if tc >= 2:
                nc.tensor.wait_ge(cSC.h, sc_v[tc - 2])
            inst = None
            for k in range(KC):
                inst = nc.tensor.matmul(p_v[tc % 2][:], ln1T[k][:, tc * 128:(tc + 1) * 128],
                                        wkvb[:, k * 2 * DH + DH:(k + 1) * 2 * DH],
                                        start=(k == 0), stop=(k == KC - 1), skip_group_check=True)
            pv = cPE.inc(inst)
            nc.scalar.wait_ge(cPE.h, pv)
            sc_v.append(cSC.inc(nc.scalar.activation(vN[:, tc * DH:(tc + 1) * DH], p_v[tc % 2][:], AF.Copy)))
        sc_v_done = sc_v[-1]
        pB_es.close()
        ar_rope.free()
        ar_ln1.free()
        ar_b.free()

        ar_att = nc.auto_arena()
        expT = [ar_att.sbuf_tensor(f"expT{i}", [128, SW], BF16) for i in range(6)]
        rbc = [ar_att.sbuf_tensor(f"rbcC{i}", [128, SW], F32) for i in range(2)]
        rrow = ar_att.sbuf_tensor("rrow", [1, SW], F32)
        rrhi = ar_att.sbuf_tensor("rrhi", [1, SW], BF16)
        xD = [ar_att.sbuf_tensor(f"xD{i}", [128, t], F32) for i in range(2)]
        wob = [ar_att.sbuf_tensor(f"wob{i}", [128, d], BF16) for i in range(2)]
        ar_att.allocate(side="left")
        ar_ao = nc.auto_arena()
        aoT = [ar_ao.sbuf_tensor(f"aoT{m}", [128, t], BF16) for m in range(h)]
        ar_ao.allocate(side="left")
        pC_es = ExitStack()
        p_sim = [pC_es.enter_context(nc.psum_tensor(f"psim{i}", [128, SW], F32)) for i in range(3)]
        p_o = [pC_es.enter_context(nc.psum_tensor(f"po{i}", [128, SW], F32)) for i in range(3)]
        p_su = [pC_es.enter_context(nc.psum_tensor(f"psu{i}", [128, SW], F32)) for i in range(2)]
        c_allones = c_mask[:, 128:256]

        nc.tensor.wait_ge(cDV.h, dv_ropek_last)
        nc.tensor.wait_ge(cSC.h, sc_v_done)

        sim_done, exp_sc, ready, av_done = [], [], [], []
        norm_done, rbc_done, pe_b_l = [], [], []
        pending_tail = [None]
        g = 0
        hs = 0

        rec_l = []

        def emit_tail_sc(bs):
            nc.scalar.wait_ge(cPE.h, last_av_of_block[bs])
            if bs >= 1:
                nc.scalar.wait_ge(cPE.h, pe_b_l[bs - 1])
            v_ln = cSC.inc(nc.scalar.activation(rrow[0:1, :], p_su[bs % 2][0:1, :], AF.Ln))
            nc.scalar.wait_ge(cSC.h, v_ln)
            rec_l.append(cSC.inc(nc.scalar.activation(rrhi[0:1, :], rrow[0:1, :], AF.Exp, scale=K_M1)))

        def emit_tail(bs, bh, bsl):
            nc.tensor.wait_ge(cSC.h, rec_l[bs])
            pe_b = cPE.inc(nc.tensor.matmul(p_su[bs % 2][:], c_onesb1[:], rrhi[0:1, :],
                                            start=True, stop=True, skip_group_check=True))
            pe_b_l.append(pe_b)
            nc.vector.wait_ge(cPE.h, pe_b)
            if bs >= 2:
                nc.vector.wait_ge(cDV.h, norm_done[bs - 2])
            rbc_done.append(cDV.inc(nc.vector.tensor_copy(rbc[bs % 2][:], p_su[bs % 2][:])))
            nc.vector.wait_ge(cDV.h, rbc_done[bs])
            norm_done.append(cDV.inc(nc.vector.tensor_mul(aoT[bh][:, bsl], p_o[bs % 3][:], rbc[bs % 2][:])))

        pend = []
        last_av_of_block = {}

        def emit_av(bs, jc, gg, first, lastc, co):
            if first and bs >= 3:
                nc.tensor.wait_ge(cDV.h, norm_done[bs - 3])
            if first and bs >= 2:
                nc.tensor.wait_ge(cDV.h, rbc_done[bs - 2])
            r = ready[gg]
            nc.tensor.wait_ge(r[1].h, r[0])
            nc.tensor.matmul(p_o[bs % 3][:, co:SW], vN[:, jc * DH:(jc + 1) * DH],
                             expT[gg % 6][:, 0:SW - co],
                             start=first, stop=lastc, skip_group_check=True)
            av_done.append(cPE.inc(nc.tensor.matmul(
                p_su[bs % 2][:, co:SW], c_allones, expT[gg % 6][:, 0:SW - co],
                start=first, stop=lastc, skip_group_check=True)))
            if lastc:
                last_av_of_block[bs] = cPE.n

        DmaQ.wait_toks(nc.vector, consts_toks)
        for hh in range(h):
            for s in range(S):
                jmax = 4 * s + 4
                sl = slice(s * SW, (s + 1) * SW)
                for jc in range(jmax):
                    diag = jc >= 4 * s
                    co = (jc - 4 * s) * 128 if diag else 0
                    w = SW - co
                    if g >= 6:
                        nc.scalar.wait_ge(cPE.h, av_done[g - 6])
                    if g >= 3:
                        nc.tensor.wait_ge(cSC.h, exp_sc[g - 3])
                    sim_done.append(cPE.inc(nc.tensor.matmul(
                        p_sim[g % 3][:, 0:w], kTt[:, jc * 128:(jc + 1) * 128],
                        qT[hh][:, s * SW + co:(s + 1) * SW],
                        start=True, stop=True, skip_group_check=True)))
                    nc.scalar.wait_ge(cPE.h, sim_done[g])
                    exp_sc.append(cSC.inc(nc.scalar.activation(
                        expT[g % 6][:, 0:w], p_sim[g % 3][:, 0:w], AF.Exp, bias=K_SHIFT_P)))
                    if diag:
                        p = jc - 4 * s
                        nc.vector.wait_ge(cSC.h, exp_sc[g])
                        mk = cDV.inc(nc.vector.tensor_mul(
                            expT[g % 6][:, 0:128], expT[g % 6][:, 0:128],
                            c_mask[:, p * SW + co:p * SW + co + 128]))
                        ready.append((mk, cDV))
                    else:
                        ready.append((exp_sc[g], cSC))
                    pend.append((hs, jc, g, jc == 0, jc == jmax - 1, co))
                    while len(pend) > 2:
                        emit_av(*pend.pop(0))
                    if jc == 1 and pending_tail[0] is not None:
                        emit_tail_sc(pending_tail[0][0])
                    if jc == 3 and pending_tail[0] is not None:
                        emit_tail(*pending_tail[0])
                        pending_tail[0] = None
                    g += 1
                pending_tail[0] = (hs, hh, sl)
                hs += 1
        while pend:
            emit_av(*pend.pop(0))
        emit_tail_sc(pending_tail[0][0])
        emit_tail(*pending_tail[0])
        c_done_dv = norm_done[-1]
        pC_es.close()
        ar_qkv.free()

        ar_wgu = nc.auto_arena()
        wgb = [ar_wgu.sbuf_tensor(f"wgb{i}", [128, d], BF16) for i in range(3)]
        wub = [ar_wgu.sbuf_tensor(f"wub{i}", [128, d], BF16) for i in range(3)]
        ar_wgu.allocate(side="right")
        ar_x2 = nc.auto_arena()
        x2b = [ar_x2.sbuf_tensor(f"x2b{k}", [128, t], BF16) for k in range(3)]
        x2m = [ar_x2.sbuf_tensor(f"x2m{i}", [128, t], F32) for i in range(2)]
        sqD = [ar_x2.sbuf_tensor(f"Dsq{i}", [128, t], BF16) for i in range(2)]
        ar_x2.allocate(side="right")
        stD_es = ExitStack()
        pD_st = [stD_es.enter_context(nc.psum_tensor(f"Dpst{s}", [128, SW], F32)) for s in range(S)]
        pD_sx = [pD_st[s][0:1, :] for s in range(S)]
        pD_sx2 = [pD_st[s][32:33, :] for s in range(S)]
        pD_es = ExitStack()
        p_d = [pD_es.enter_context(nc.psum_tensor(f"pd{s}", [128, SW], F32)) for s in range(S)]

        Wout_r = Wout.rearrange("(k p) (m c) -> m p k c", p=128, c=128)
        WgT_r = WgT.rearrange("(k p) (m c) -> m p k c", p=128, c=128)
        WuT_r = WuT.rearrange("(k p) (m c) -> m p k c", p=128, c=128)
        wo_ld, pe_d, dv_add, st_x2 = [], [], [], []
        d_sc, d_pe = [], []
        wg_ld, wu_ld = [], []

        def emit_stats_D(m):
            nc.tensor.wait_ge(cSC.h, d_sc[m])
            last = None
            for s in range(S):
                sl = slice(s * SW, (s + 1) * SW)
                nc.tensor.matmul(pD_sx[s], c_ones[:], x2b[m % 3][:, sl],
                                 start=(m == 0), stop=(m == KC - 1), skip_group_check=True)
                last = nc.tensor.matmul(pD_sx2[s], c_ones[:], sqD[m % 2][:, sl],
                                        start=(m == 0), stop=(m == KC - 1), skip_group_check=True)
            d_pe.append(cPE.inc(last))

        v_ad_tok = []
        for m in range(KC):
            if m >= 2:
                nc.gpsimd.wait_ge(cPE.h, pe_d[m - 2][S - 1])
            if m == 0:
                nc.gpsimd.wait_ge(cDV.h, c_done_dv)
            dst = wob[m % 2].ap().rearrange("p (k c) -> p k c", c=128)
            wo_ld.append(qW.dma(dst, Wout_r[m]))
            if m % 2 == 0 and 2 <= m <= 12:
                j = (m - 2) // 2
                hcp, up_half = j // 2, j % 2
                if up_half == 0:
                    dstp = wgb[hcp].ap().rearrange("p (k c) -> p k c", c=128)
                    wg_ld.append(qW.dma(dstp, WgT_r[hcp]))
                else:
                    dstp = wub[hcp].ap().rearrange("p (k c) -> p k c", c=128)
                    wu_ld.append(qW.dma(dstp, WuT_r[hcp]))
            if m == 0:
                nc.sync.wait_ge(cSC.h, sc_v_done)
                nc.sync.wait_ge(cDV.h, dv_ropek_last)
            if m >= 2:
                nc.sync.wait_ge(cDV.h, dv_add[m - 2])
            ld_m = qL.dma(xD[m % 2][:], xT[m * 128:(m + 1) * 128, :])
            wtok(nc.tensor, wo_ld[m])
            if m == 0:
                nc.tensor.wait_ge(cDV.h, c_done_dv)
            grp = []
            toks = []
            for s in range(S):
                sl = slice(s * SW, (s + 1) * SW)
                if m >= 1:
                    nc.tensor.wait_ge(cDV.h, v_ad_tok[m - 1][s])
                inst = None
                for k in range(KC):
                    inst = nc.tensor.matmul(p_d[s][:], wob[m % 2][:, k * 128:(k + 1) * 128],
                                            aoT[k][:, sl],
                                            start=(k == 0), stop=(k == KC - 1), skip_group_check=True)
                grp.append(cPE.inc(inst))
                if s == 0:
                    wtok(nc.vector, ld_m)
                    if m >= 2:
                        wtok(nc.vector, st_x2[m - 2])
                nc.vector.wait_ge(cPE.h, grp[s])
                toks.append(cDV.inc(nc.vector.tensor_add(
                    x2m[m % 2][:, sl], p_d[s][:], xD[m % 2][:, sl])))
            pe_d.append(grp)
            v_ad_tok.append(toks)
            if m >= 1:
                emit_stats_D(m - 1)
            nc.vector.wait_ge(cDV.h, toks[S - 1])
            if m >= 3:
                nc.vector.wait_ge(cPE.h, d_pe[m - 3])
            dv_add.append(cDV.inc(nc.vector.tensor_copy(x2b[m % 3][:], x2m[m % 2][:])))
            nc.scalar.wait_ge(cDV.h, dv_add[m])
            st_x2.append(qS.dma(x2_scr[m * 128:(m + 1) * 128, :], x2m[m % 2][:]))
            if m >= 2:
                nc.scalar.wait_ge(cPE.h, d_pe[m - 2])
            d_sc.append(cSC.inc(nc.scalar.activation(sqD[m % 2][:], x2b[m % 3][:], AF.Square)))
        emit_stats_D(KC - 1)
        pD_es.close()
        ar_ao.free()

        ar_dst = nc.auto_arena()
        meanD = ar_dst.sbuf_tensor("Dmbc", [128, t], BF16)
        rstdD = ar_dst.sbuf_tensor("Drbc", [128, t], BF16)
        tmpD = ar_dst.sbuf_tensor("Dtmp", [128, t], BF16)
        rowD = [ar_dst.sbuf_tensor(f"DrowF{i}", [1, t], F32) for i in range(3)]
        ar_dst.allocate(side="left")
        rwD = ([rowD[0][0:1, :], rowD[1][0:1, :], rowD[2][0:1, :]]
               + [sqD[0][0:1, :], sqD[1][0:1, :]])
        dv_bcD, pe_bcD = ln_finalize_bcast(rwD, meanD, rstdD, pD_sx, pD_sx2, d_pe[-1], stD_es, "D")
        ar_x2.free()
        ar_ln2 = nc.auto_arena()
        ln2T = [ar_ln2.sbuf_tensor(f"ln2T{k}", [128, t], BF16) for k in range(KC)]
        ar_ln2.allocate(side="right")
        d_ap = []
        DmaQ.wait_toks(nc.vector, st_x2)
        for m in range(KC):
            if m < 2:
                nc.sync.wait_ge(cDV.h, dv_add[KC - 1])
            if m >= 2:
                nc.sync.wait_ge(cDV.h, d_ap[m - 2])
            wtok(nc.sync, st_x2[m])
            t_ap2 = qL.dma(xD[m % 2][:], x2_scr[m * 128:(m + 1) * 128, :])
            wtok(nc.vector, t_ap2)
            if m == 0:
                nc.vector.wait_ge(cDV.h, dv_bcD)
            if m >= 1:
                nc.vector.wait_ge(cDV.h, d_ap[m - 1])
            v_sub = cDV.inc(nc.vector.tensor_sub(tmpD[:], xD[m % 2][:], meanD[:]))
            nc.vector.wait_ge(cDV.h, v_sub)
            d_ap.append(cDV.inc(nc.vector.tensor_mul(ln2T[m][:], tmpD[:], rstdD[:])))
        ln2_done = d_ap[-1]
        ar_dst.free()
        ar_att.free()

        ar_fpre = nc.auto_arena()
        wdb = [ar_fpre.sbuf_tensor(f"wdb{i}", [128, d], BF16) for i in range(WDR)]
        gtb = [ar_fpre.sbuf_tensor(f"gtb{i}", [128, FSLAB * TH], BF16) for i in range(2)]
        ar_fpre.allocate(side="left")
        ar_e = nc.auto_arena()
        sgt = ar_e.sbuf_tensor("sgt", [128, t], F32)
        gt = [ar_e.sbuf_tensor(f"gt{i}", [128, t], BF16) for i in range(2)]
        ar_e.allocate(side="right")
        pE_es = ExitStack()
        p_g = pE_es.enter_context(nc.psum_tensor("pg", [128, t], F32))
        p_u = pE_es.enter_context(nc.psum_tensor("pu", [128, t], F32))

        pe_g, pe_u = [], []
        sc_silu, dv_mult, st_g = [], [], []
        wd_ld = []
        gwd = 0
        for hc in range(HC):
            if hc >= 3:
                nc.gpsimd.wait_ge(cPE.h, pe_g[hc - 3])
                dst = wgb[hc % 3].ap().rearrange("p (k c) -> p k c", c=128)
                wg_ld.append(qW.dma(dst, WgT_r[hc]))
                nc.gpsimd.wait_ge(cPE.h, pe_u[hc - 3])
                dst = wub[hc % 3].ap().rearrange("p (k c) -> p k c", c=128)
                wu_ld.append(qW.dma(dst, WuT_r[hc]))
            if 16 <= hc < 48 and hc % 2 == 0:
                if gwd == 0:
                    nc.gpsimd.wait_ge(cDV.h, ln2_done)
                wd_ld.append(qW.dma(wdb[gwd % WDR][:], WdT[gwd * 128:(gwd + 1) * 128, :]))
                gwd += 1
            wtok(nc.tensor, wg_ld[hc])
            if hc == 0:
                wtok(nc.tensor, wu_ld[0])
                for k in range(KC):
                    nc.tensor.wait_ge(cDV.h, d_ap[k])
                    for pp, buf in ((p_g, wgb), (p_u, wub)):
                        for s in range(S):
                            sl = slice(s * SW, (s + 1) * SW)
                            inst = nc.tensor.matmul(pp[:, sl], buf[0][:, k * 128:(k + 1) * 128],
                                                    ln2T[k][:, sl],
                                                    start=(k == 0), stop=(k == KC - 1),
                                                    skip_group_check=True)
                            if k == KC - 1:
                                if pp is p_g and s == S - 1:
                                    pe_g.append(cPE.inc(inst))
                                elif pp is p_u and s == S - 1:
                                    pe_u.append(cPE.inc(inst))
            else:
                nc.tensor.wait_ge(cSC.h, sc_silu[hc - 1])
                inst = None
                for k in range(KC):
                    for s in range(S):
                        sl = slice(s * SW, (s + 1) * SW)
                        inst = nc.tensor.matmul(p_g[:, sl], wgb[hc % 3][:, k * 128:(k + 1) * 128],
                                                ln2T[k][:, sl],
                                                start=(k == 0), stop=(k == KC - 1), skip_group_check=True)
                pe_g.append(cPE.inc(inst))
                wtok(nc.tensor, wu_ld[hc])
                nc.tensor.wait_ge(cDV.h, dv_mult[hc - 1])
                for k in range(KC):
                    for s in range(S):
                        sl = slice(s * SW, (s + 1) * SW)
                        inst = nc.tensor.matmul(p_u[:, sl], wub[hc % 3][:, k * 128:(k + 1) * 128],
                                                ln2T[k][:, sl],
                                                start=(k == 0), stop=(k == KC - 1), skip_group_check=True)
                pe_u.append(cPE.inc(inst))
            nc.scalar.wait_ge(cPE.h, pe_g[hc])
            if hc >= 1:
                nc.scalar.wait_ge(cDV.h, dv_mult[hc - 1])
            sc_silu.append(cSC.inc(nc.scalar.activation(sgt[:], p_g[:], AF.Silu)))
            nc.vector.wait_ge(cSC.h, sc_silu[hc])
            nc.vector.wait_ge(cPE.h, pe_u[hc])
            if hc >= 2:
                wtok(nc.vector, st_g[hc - 2])
            dv_mult.append(cDV.inc(nc.vector.tensor_mul(gt[hc % 2][:], sgt[:], p_u[:])))
            nc.scalar.wait_ge(cDV.h, dv_mult[hc])
            st_g.append(qS.dma(g_scr[hc * 128:(hc + 1) * 128, :], gt[hc % 2][:]))
        pE_es.close()
        ar_e.free()
        ar_ln2.free()
        ar_wgu.free()

        ar_acc = nc.auto_arena()
        acc = [ar_acc.sbuf_tensor(f"acc{m}", [128, TH], F32) for m in range(KC)]
        x2r = [ar_acc.sbuf_tensor(f"x2r{i}", [128, TH], F32) for i in range(4)]
        ar_acc.allocate(side="left")
        pF_es = ExitStack()
        p_f = [pF_es.enter_context(nc.psum_tensor(f"pf{i}", [128, TH], F32)) for i in range(2)]

        g_scr_r = g_scr.rearrange("(n p) tt -> p n tt", p=128)

        order = ([(0, s) for s in range(NSLAB)]
                 + [(1, NSLAB - 2), (1, NSLAB - 1)]
                 + [(1, s) for s in range(NSLAB - 3, -1, -1)])
        gt_ld, slab_pe = [], []
        dv_f_hist, st_out = [], []
        TH_S = TH // SW
        wd_toks_for = {0: wd_ld[0:FSLAB], 1: wd_ld[FSLAB:2 * FSLAB]}
        for gs, (hf, slab) in enumerate(order):
            DmaQ.wait_toks(nc.sync, st_g[:(slab + 1) * FSLAB])
            if gs >= 2:
                nc.sync.wait_ge(cPE.h, slab_pe[gs - 2])
            dst = gtb[gs % 2].ap().rearrange("p (n tt) -> p n tt", tt=TH)
            src = g_scr_r[:, slab * FSLAB:(slab + 1) * FSLAB, hf * TH:(hf + 1) * TH]
            gt_ld.append(qL.dma(dst, src))
            nx = gs + 1
            if nx < len(order) and nx >= 2 and nx not in (8, 9):
                toks = []
                for hcl in range(FSLAB):
                    hc = order[nx][1] * FSLAB + hcl
                    if gs >= 1:
                        nc.gpsimd.wait_ge(cPE.h, slab_pe[gs - 1])
                    toks.append(qW.dma(wdb[(nx % 2) * FSLAB + hcl][:],
                                       WdT[hc * 128:(hc + 1) * 128, :]))
                wd_toks_for[nx] = toks
            if gs in wd_toks_for:
                DmaQ.wait_toks(nc.tensor, wd_toks_for[gs])
            wtok(nc.tensor, gt_ld[gs])
            if gs == 0:
                nc.tensor.wait_ge(cDV.h, dv_mult[-1])
            first_of_half = (gs % NSLAB == 0)
            last_of_half = (gs % NSLAB == NSLAB - 1)
            for m in range(KC):
                if len(dv_f_hist) >= 2:
                    nc.tensor.wait_ge(cDV.h, dv_f_hist[-2])
                inst = None
                for hcl in range(FSLAB):
                    wt = wdb[(gs % 2) * FSLAB + hcl]
                    for s2 in range(TH_S):
                        sl = slice(s2 * SW, (s2 + 1) * SW)
                        inst = nc.tensor.matmul(
                            p_f[m % 2][:, sl], wt[:, m * 128:(m + 1) * 128],
                            gtb[gs % 2][:, hcl * TH + s2 * SW:hcl * TH + (s2 + 1) * SW],
                            start=(hcl == 0), stop=(hcl == FSLAB - 1), skip_group_check=True)
                pe_fm = cPE.inc(inst)
                if last_of_half:
                    qbase = hf * KC

                    def issue_x2r(mm):
                        qq = qbase + mm
                        qx = qL if mm % 2 == 0 else qW
                        if qq >= 4:
                            wtok(qx.eng, st_out[qq - 4])
                        x2r_toks[mm] = qx.dma(
                            x2r[qq % 4][:],
                            x2_scr[mm * 128:(mm + 1) * 128, hf * TH:(hf + 1) * TH])

                    if m == 0:
                        x2r_toks = {}
                        if hf == 0:
                            DmaQ.wait_toks(nc.sync, st_g)
                            DmaQ.wait_toks(nc.gpsimd, st_g)
                        issue_x2r(0)
                        issue_x2r(1)
                    if m + 2 < KC:
                        issue_x2r(m + 2)
                nc.vector.wait_ge(cPE.h, pe_fm)
                if first_of_half:
                    dv_f = cDV.inc(nc.vector.tensor_copy(acc[m][:], p_f[m % 2][:]))
                elif not last_of_half:
                    dv_f = cDV.inc(nc.vector.tensor_add(acc[m][:], acc[m][:], p_f[m % 2][:]))
                else:
                    xsl = x2r[(hf * KC + m) % 4]
                    v_ac = cDV.inc(nc.vector.tensor_add(acc[m][:], acc[m][:], p_f[m % 2][:]))
                    wtok(nc.vector, x2r_toks[m])
                    nc.vector.wait_ge(cDV.h, v_ac)
                    dv_f = cDV.inc(nc.vector.tensor_add(xsl[:], acc[m][:], xsl[:]))
                    if hf == 1 and m >= KC - 2:
                        HT2 = TH // 2
                        qa, qb2 = (qS, qL) if m % 2 == 0 else (qW, qS)
                        for qx, c0 in ((qa, 0), (qb2, HT2)):
                            qx.eng.wait_ge(cDV.h, dv_f)
                            st_out.append(qx.dma(
                                outT[m * 128:(m + 1) * 128, hf * TH + c0:hf * TH + c0 + HT2],
                                xsl[:, c0:c0 + HT2]))
                    else:
                        stq = qS if (hf == 0 or m % 2 == 0) else qW
                        stq.eng.wait_ge(cDV.h, dv_f)
                        st_out.append(stq.dma(outT[m * 128:(m + 1) * 128, hf * TH:(hf + 1) * TH],
                                              xsl[:]))
                dv_f_hist.append(dv_f)
            slab_pe.append(cPE.n)
        pF_es.close()
        ar_acc.free()
        ar_fpre.free()

        qS.wait_all(nc.scalar)
        qL.wait_all(nc.sync)
        qW.wait_all(nc.gpsimd)
    return nc


def _host_consts(t=T, d=D):
    inv_freq = (1.0 / (10000.0 ** (np.arange(0, DH, 2, dtype=np.float32) / np.float32(DH)))).astype(np.float32)
    pos = np.arange(t, dtype=np.float32)
    freqs = pos[None, :] * inv_freq[:, None]
    cos = np.cos(np.concatenate([freqs, freqs], 0)).astype(np.float32)
    sin = np.sin(np.concatenate([freqs, freqs], 0)).astype(np.float32)
    sgn = np.concatenate([-np.ones((64, t), np.float32), np.ones((64, t), np.float32)], 0)
    scale = np.float32(DH ** -0.5)
    ropeQc = (cos * scale).astype(np.float32)
    ropeQs = np.roll((sin * sgn * scale).astype(np.float32), 64, axis=0)
    ropeKc = cos
    ropeKs = np.roll((sin * sgn).astype(np.float32), 64, axis=0)
    m = np.zeros((128, 4 * SW), np.float32)
    for p in range(4):
        jr = np.arange(128)[:, None]
        ic = np.arange(SW)[None, :]
        m[:, p * SW:(p + 1) * SW] = (ic >= p * 128 + jr)
    masks = m.astype(ml_dtypes.bfloat16)
    onesb = np.ones((128, 1), ml_dtypes.bfloat16)
    onesf = np.ones((1, 128), np.float32)
    onesb1 = np.ones((1, 128), ml_dtypes.bfloat16)
    cstv = np.zeros((128, 8), np.float32)
    cstv[:, 0] = np.float32(1.0) / np.float32(d)
    cstv[:, 1] = np.float32(EPS)
    cstv[:, 2] = np.float32(-0.5)
    cstv[:, 3] = np.float32(-1.0)
    cstv[:, 4] = np.float32(-EXP_SHIFT)
    return dict(ropeQc=ropeQc, ropeQs=ropeQs, ropeKc=ropeKc, ropeKs=ropeKs,
                masks=masks, onesb=onesb, onesf=onesf, onesb1=onesb1, cst=cstv)


def kernel(L0_attn_x, L0_attn_Wq, L0_attn_Wkv, L0_attn_Wout,
           L0_ffn_w_gate, L0_ffn_w_up, L0_ffn_w_down):
    x = np.asarray(L0_attn_x, dtype=np.float32)
    B = x.shape[0]
    consts = _host_consts()
    Wq_h = np.ascontiguousarray(np.asarray(L0_attn_Wq, dtype=np.float32))
    Wkv_h = np.ascontiguousarray(np.asarray(L0_attn_Wkv, dtype=np.float32))
    Wout_h = np.ascontiguousarray(np.asarray(L0_attn_Wout, dtype=np.float32))
    WgT_h = np.ascontiguousarray(np.asarray(L0_ffn_w_gate, dtype=np.float32).T)
    WuT_h = np.ascontiguousarray(np.asarray(L0_ffn_w_up, dtype=np.float32).T)
    WdT_h = np.ascontiguousarray(np.asarray(L0_ffn_w_down, dtype=np.float32).T)

    nc = bass.Bass("TRN2", target_bir_lowering=False)
    build(nc)

    in_maps = []
    for c in range(B):
        m = dict(consts)
        m["xT"] = np.ascontiguousarray(x[c].T)
        m["Wq"] = Wq_h
        m["Wkv"] = Wkv_h
        m["Wout"] = Wout_h
        m["WgT"] = WgT_h
        m["WuT"] = WuT_h
        m["WdT"] = WdT_h
        in_maps.append(m)
    res = run_bass_kernel_spmd(nc, in_maps, core_ids=list(range(B)))
    out = np.stack([np.asarray(res.results[c]["outT"]).T for c in range(B)], axis=0)
    return np.ascontiguousarray(out.astype(np.float32))


# version 25
# speedup vs baseline: 1.0876x; 1.0876x over previous
from contextlib import ExitStack

import numpy as np
import ml_dtypes

import concourse.bass as bass
import concourse.mybir as mybir
from concourse.bass_utils import run_bass_kernel_spmd

F32 = mybir.dt.float32
BF16 = mybir.dt.bfloat16
AF = mybir.ActivationFunctionType
OP = mybir.AluOpType

D = 2048
T = 2048
H = 16
DH = 128
HID = 8192
SW = 512
EXP_SHIFT = 4.0
EPS = 1e-5


class Sem:
    def __init__(self, nc, es, name):
        self.h = es.enter_context(nc.semaphore(name))
        self.n = 0

    def inc(self, inst, k=1):
        inst.then_inc(self.h, k)
        self.n += k
        return self.n


class DmaQ:

    def __init__(self, nc, es, name, eng, k=4):
        self.eng = eng
        self.sems = [Sem(nc, es, f"{name}{i}") for i in range(k)]
        self.k = k
        self.i = 0
        self.toks = []

    def dma(self, out, in_):
        s = self.sems[self.i % self.k]
        if s.n > 0:
            self.eng.wait_ge(s.h, s.n)
        inst = self.eng.dma_start(out, in_)
        s.inc(inst, 16)
        tok = (s, s.n)
        self.toks.append(tok)
        self.i += 1
        return tok

    def wait_all(self, eng):
        for s in self.sems:
            if s.n > 0:
                eng.wait_ge(s.h, s.n)

    @staticmethod
    def wait_toks(eng, toks):
        best = {}
        for s, v in toks:
            if v > best.get(id(s), (None, -1))[1]:
                best[id(s)] = (s, v)
        for s, v in best.values():
            eng.wait_ge(s.h, v)


def wtok(eng, tok):
    eng.wait_ge(tok[0].h, tok[1])


def build(nc, d=D, t=T, h=H, hid=HID):
    KC = d // 128
    TC = t // 128
    S = t // SW
    HC = hid // 128
    TH = t // 2
    FSLAB = 8
    NSLAB = HC // FSLAB
    WDR = 2 * FSLAB
    assert h * DH == d and t % SW == 0 and TH % SW == 0

    xT = nc.dram_tensor("xT", [d, t], F32, kind="ExternalInput")
    Wq = nc.dram_tensor("Wq", [d, d], F32, kind="ExternalInput")
    Wkv = nc.dram_tensor("Wkv", [d, 2 * DH], F32, kind="ExternalInput")
    Wout = nc.dram_tensor("Wout", [d, d], F32, kind="ExternalInput")
    WgT = nc.dram_tensor("WgT", [d, hid], F32, kind="ExternalInput")
    WuT = nc.dram_tensor("WuT", [d, hid], F32, kind="ExternalInput")
    WdT = nc.dram_tensor("WdT", [hid, d], F32, kind="ExternalInput")
    ropeQc = nc.dram_tensor("ropeQc", [128, t], F32, kind="ExternalInput")
    ropeQs = nc.dram_tensor("ropeQs", [128, t], F32, kind="ExternalInput")
    ropeKc = nc.dram_tensor("ropeKc", [128, t], F32, kind="ExternalInput")
    ropeKs = nc.dram_tensor("ropeKs", [128, t], F32, kind="ExternalInput")
    masks = nc.dram_tensor("masks", [128, 4 * SW], BF16, kind="ExternalInput")
    onesb = nc.dram_tensor("onesb", [128, 1], BF16, kind="ExternalInput")
    onesf = nc.dram_tensor("onesf", [1, 128], F32, kind="ExternalInput")
    onesb1 = nc.dram_tensor("onesb1", [1, 128], BF16, kind="ExternalInput")
    cst = nc.dram_tensor("cst", [128, 8], F32, kind="ExternalInput")
    x2_scr = nc.dram_tensor("x2_scr", [d, t], F32)
    g_scr = nc.dram_tensor("g_scr", [hid, t], BF16)
    wd_scr = nc.dram_tensor("wd_scr", [hid, d], BF16)
    outT = nc.dram_tensor("outT", [d, t], F32, kind="ExternalOutput")

    es = ExitStack()
    with es:
        qW = DmaQ(nc, es, "qW", nc.gpsimd, 8)
        qL = DmaQ(nc, es, "qL", nc.sync, 8)
        qS = DmaQ(nc, es, "qS", nc.scalar, 8)
        cPE = Sem(nc, es, "cPE")
        cDV = Sem(nc, es, "cDV")
        cSC = Sem(nc, es, "cSC")
        cGP = Sem(nc, es, "cGP")

        ar_consts = nc.auto_arena()
        c_ones = ar_consts.sbuf_tensor("c_ones", [128, 1], BF16)
        c_onesf = ar_consts.sbuf_tensor("c_onesf", [1, 128], F32)
        c_onesb1 = ar_consts.sbuf_tensor("c_onesb1", [1, 128], BF16)
        c_cst = ar_consts.sbuf_tensor("c_cst", [128, 8], F32)
        c_mask = ar_consts.sbuf_tensor("c_mask", [128, 4 * SW], BF16)
        ar_consts.allocate(side="left")

        ar_b = nc.auto_arena()
        wqb = [ar_b.sbuf_tensor(f"wqb{i}", [128, d], BF16) for i in range(2)]
        wkvb = ar_b.sbuf_tensor("wkvb", [128, KC * 2 * DH], BF16)
        rq = [ar_b.sbuf_tensor(f"rq{i}", [128, SW], BF16) for i in range(2)]
        rt1 = [ar_b.sbuf_tensor(f"rt1{i}", [128, SW], BF16) for i in range(2)]
        rt2 = [ar_b.sbuf_tensor(f"rt2{i}", [128, SW], BF16) for i in range(2)]
        ar_b.allocate(side="left")

        ar_ln1 = nc.auto_arena()
        ln1T = [ar_ln1.sbuf_tensor(f"ln1T{k}", [128, t], BF16) for k in range(KC)]
        ar_ln1.allocate(side="left")

        ar_rope = nc.auto_arena()
        r_qc = ar_rope.sbuf_tensor("r_qc", [128, t], F32)
        r_qs = ar_rope.sbuf_tensor("r_qs", [128, t], F32)
        r_kc = ar_rope.sbuf_tensor("r_kc", [128, t], F32)
        r_ks = ar_rope.sbuf_tensor("r_ks", [128, t], F32)
        ar_rope.allocate(side="left")

        ar_a = nc.auto_arena()
        xA = [ar_a.sbuf_tensor(f"AxA{i}", [128, t], F32) for i in range(5)]
        sqb = [ar_a.sbuf_tensor(f"Asq{i}", [128, t], BF16) for i in range(2)]
        tmpA = ar_a.sbuf_tensor("Atmp", [128, t], BF16)
        meanA = ar_a.sbuf_tensor("Ambc", [128, t], BF16)
        rstdA = ar_a.sbuf_tensor("Arbc", [128, t], BF16)
        rowA = [ar_a.sbuf_tensor(f"ArowF{i}", [1, t], F32) for i in range(3)]
        ar_a.allocate(side="left")
        rwA = ([rowA[0][0:1, :], rowA[1][0:1, :], rowA[2][0:1, :]]
               + [sqb[0][0:1, :], sqb[1][0:1, :]])

        for dst, src in ((c_ones, onesb), (c_onesf, onesf), (c_onesb1, onesb1),
                         (c_cst, cst), (c_mask, masks)):
            qS.dma(dst[:], src[:])
        consts_toks = list(qS.toks)
        K_1_D = c_cst[0:1, 0:1]
        K_EPS = c_cst[0:1, 1:2]
        K_M05 = c_cst[0:1, 2:3]
        K_M1 = c_cst[0:1, 3:4]
        K_SHIFT_P = c_cst[:, 4:5]

        def ln_finalize_bcast(rows, mean_bc, rstd_bc, p_sx, p_sx2, pe_last, st_es, tag):
            rw, rw2, rw3, rbh_m, rbh_r = rows
            DmaQ.wait_toks(nc.scalar, consts_toks)
            nc.scalar.wait_ge(cPE.h, pe_last)
            sc_fin = None
            for s in range(S):
                sl = slice(s * SW, (s + 1) * SW)
                nc.scalar.activation(rw[0:1, sl], p_sx[s], AF.Copy, scale=K_1_D)
                nc.scalar.activation(rw3[0:1, sl], p_sx2[s], AF.Copy, scale=K_1_D)
                sc_fin = cSC.inc(nc.scalar.activation(rw2[0:1, sl], p_sx[s], AF.Square, scale=K_1_D))
            nc.vector.wait_ge(cSC.h, sc_fin)
            dv_var = None
            for s in range(S):
                sl = slice(s * SW, (s + 1) * SW)
                dv_var = cDV.inc(nc.vector.tensor_sub(rw2[0:1, sl], rw3[0:1, sl], rw2[0:1, sl]))
            nc.scalar.wait_ge(cDV.h, dv_var)
            v_ln = cSC.inc(nc.scalar.activation(rw3[0:1, :], rw2[0:1, :], AF.Ln, bias=K_EPS))
            nc.scalar.wait_ge(cSC.h, v_ln)
            sc_rstd = cSC.inc(nc.scalar.activation(rbh_r[0:1, :], rw3[0:1, :], AF.Exp, scale=K_M05))
            nc.vector.wait_ge(cSC.h, sc_rstd)
            dv_hi = cDV.inc(nc.vector.tensor_copy(rbh_m[0:1, :], rw[0:1, :]))
            st_es.close()
            bc_es = ExitStack()
            p_bc1 = bc_es.enter_context(nc.psum_tensor(f"{tag}pbc1", [128, t], F32))
            p_bc2 = bc_es.enter_context(nc.psum_tensor(f"{tag}pbc2", [128, t], F32))
            nc.tensor.wait_ge(cDV.h, dv_hi)
            nc.tensor.wait_ge(cSC.h, sc_rstd)
            pe_bc = None
            for s in range(S):
                sl = slice(s * SW, (s + 1) * SW)
                nc.tensor.matmul(p_bc1[:, sl], c_onesb1[:], rbh_m[0:1, sl], start=True, stop=True,
                                 skip_group_check=True)
                pe_bc = cPE.inc(nc.tensor.matmul(p_bc2[:, sl], c_onesb1[:], rbh_r[0:1, sl],
                                                 start=True, stop=True, skip_group_check=True))
            nc.vector.wait_ge(cPE.h, pe_bc)
            nc.vector.tensor_copy(mean_bc[:], p_bc1[:])
            dv_bc = cDV.inc(nc.vector.tensor_copy(rstd_bc[:], p_bc2[:]))
            bc_es.close()
            return dv_bc, pe_bc

        stA_es = ExitStack()
        pA_st = [stA_es.enter_context(nc.psum_tensor(f"Apst{s}", [128, SW], F32)) for s in range(S)]
        pA_sx = [pA_st[s][0:1, :] for s in range(S)]
        pA_sx2 = [pA_st[s][32:33, :] for s in range(S)]
        a_dv, a_sc, a_pe = [], [], []
        a_ld = [None] * KC

        def issue_a(m):
            q = {0: qL, 1: qL, 2: qS, 3: qW, 4: qS}.get(m, qW if m % 2 == 1 else qL)
            if m >= 5:
                q.eng.wait_ge(cDV.h, a_dv[m - 5])
                sc_sem, sc_val = a_sc[m - 5]
                q.eng.wait_ge(sc_sem.h, sc_val)
            a_ld[m] = q.dma(xA[m % 5][:], xT[m * 128:(m + 1) * 128, :])

        for m in range(5):
            issue_a(m)
        rope_toks_q = [qS.dma(r_qc[:], ropeQc[:]), qS.dma(r_qs[:], ropeQs[:])]
        for m in range(KC):
            wtok(nc.vector, a_ld[m])
            a_dv.append(cDV.inc(nc.vector.tensor_copy(ln1T[m][:], xA[m % 5][:])))
            if m % 2 == 0:
                wtok(nc.scalar, a_ld[m])
                if m >= 2:
                    nc.scalar.wait_ge(cPE.h, a_pe[m - 2])
                a_sc.append((cSC, cSC.inc(nc.scalar.activation(sqb[m % 2][:], xA[m % 5][:], AF.Square))))
            else:
                nc.vector.wait_ge(cDV.h, a_dv[m])
                if m >= 2:
                    nc.vector.wait_ge(cPE.h, a_pe[m - 2])
                a_sc.append((cDV, cDV.inc(nc.vector.tensor_mul(sqb[m % 2][:], ln1T[m][:], ln1T[m][:]))))
            if m == 0:
                DmaQ.wait_toks(nc.tensor, consts_toks)
            nc.tensor.wait_ge(cDV.h, a_dv[m])
            sc_sem, sc_val = a_sc[m]
            nc.tensor.wait_ge(sc_sem.h, sc_val)
            last = None
            for s in range(S):
                sl = slice(s * SW, (s + 1) * SW)
                nc.tensor.matmul(pA_sx[s], c_ones[:], ln1T[m][:, sl],
                                 start=(m == 0), stop=(m == KC - 1), skip_group_check=True)
                last = nc.tensor.matmul(pA_sx2[s], c_ones[:], sqb[m % 2][:, sl],
                                        start=(m == 0), stop=(m == KC - 1), skip_group_check=True)
            a_pe.append(cPE.inc(last))
            if m + 5 < KC:
                issue_a(m + 5)
        wq_ld = []
        Wq_r = Wq.rearrange("(k p) (m c) -> m p k c", p=128, c=128)
        for i in range(2):
            dstw = wqb[i].ap().rearrange("p (k c) -> p k c", c=128)
            wq_ld.append(qW.dma(dstw, Wq_r[i]))
        rope_toks_k = [qW.dma(r_kc[:], ropeKc[:]), qW.dma(r_ks[:], ropeKs[:])]
        dv_bcA, pe_bcA = ln_finalize_bcast(rwA, meanA, rstdA, pA_sx, pA_sx2, a_pe[-1], stA_es, "A")
        a_ap = []
        for m in range(KC):
            if m == 0:
                nc.vector.wait_ge(cDV.h, dv_bcA)
            if m >= 1:
                nc.vector.wait_ge(cDV.h, a_ap[m - 1])
            v_sub = cDV.inc(nc.vector.tensor_sub(tmpA[:], ln1T[m][:], meanA[:]))
            nc.vector.wait_ge(cDV.h, v_sub)
            a_ap.append(cDV.inc(nc.vector.tensor_mul(ln1T[m][:], tmpA[:], rstdA[:])))
        ln1_done = a_ap[-1]
        ar_a.free()

        ar_qkv = nc.auto_arena()
        qT = [ar_qkv.sbuf_tensor(f"qT{m}", [128, t], BF16) for m in range(h)]
        kTt = ar_qkv.sbuf_tensor("kTt", [128, t], BF16)
        vN = ar_qkv.sbuf_tensor("vN", [128, TC * DH], BF16)
        ar_qkv.allocate(side="right")
        pB_es = ExitStack()
        p_q = [pB_es.enter_context(nc.psum_tensor(f"pq{s}", [128, SW], F32)) for s in range(S)]
        p_v = [pB_es.enter_context(nc.psum_tensor(f"pv{i}", [128, DH], F32)) for i in range(2)]

        Wkv_r = Wkv.rearrange("(k p) c -> p k c", p=128)

        wkv_tok = [None]
        pe_q = []
        sc_rope, dv_rope, gp_rope = [], [], []
        pe_all_rope = []

        def emit_wq_slab(m):
            if m >= 2:
                nc.gpsimd.wait_ge(cPE.h, pe_q[(m - 2) * S + S - 1])
            dst = wqb[m % 2].ap().rearrange("p (k c) -> p k c", c=128)
            wq_ld.append(qW.dma(dst, Wq_r[m]))

        def emit_rope(idx, src_psum, dstT, dst_sl, cosT, sinT):
            nc.scalar.wait_ge(cPE.h, pe_all_rope[idx])
            if idx >= 2:
                nc.scalar.wait_ge(cDV.h, dv_rope[idx - 2])
            sc_rope.append(cSC.inc(nc.scalar.activation(rq[idx % 2][:], src_psum[:], AF.Copy)))
            nc.vector.wait_ge(cSC.h, sc_rope[idx])
            if idx >= 2:
                nc.vector.wait_ge(cDV.h, dv_rope[idx - 2])
            v_t1 = cDV.inc(nc.vector.tensor_mul(rt1[idx % 2][:], rq[idx % 2][:], cosT[:, dst_sl]))
            nc.gpsimd.wait_ge(cSC.h, sc_rope[idx])
            if idx >= 2:
                nc.gpsimd.wait_ge(cDV.h, dv_rope[idx - 2])
            nc.gpsimd.tensor_tensor(rt2[idx % 2][0:64, :], rq[idx % 2][64:128, :],
                                    sinT[64:128, dst_sl], OP.mult)
            gp_rope.append(cGP.inc(nc.gpsimd.tensor_tensor(
                rt2[idx % 2][64:128, :], rq[idx % 2][0:64, :], sinT[0:64, dst_sl], OP.mult)))
            nc.vector.wait_ge(cGP.h, gp_rope[idx])
            nc.vector.wait_ge(cDV.h, v_t1)
            dv_rope.append(cDV.inc(nc.vector.tensor_add(dstT[:, dst_sl], rt1[idx % 2][:], rt2[idx % 2][:])))

        ridx = 0
        for m in range(h):
            wtok(nc.tensor, wq_ld[m])
            if m == 0:
                for k in range(KC):
                    nc.tensor.wait_ge(cDV.h, a_ap[k])
                    for s in range(S):
                        inst = nc.tensor.matmul(p_q[s][:], wqb[0][:, k * 128:(k + 1) * 128],
                                                ln1T[k][:, s * SW:(s + 1) * SW],
                                                start=(k == 0), stop=(k == KC - 1),
                                                skip_group_check=True)
                        if k == KC - 1:
                            pe_q.append(cPE.inc(inst))
            else:
                for s in range(S):
                    nc.tensor.wait_ge(cSC.h, sc_rope[(m - 1) * S + s])
                    inst = None
                    for k in range(KC):
                        inst = nc.tensor.matmul(p_q[s][:], wqb[m % 2][:, k * 128:(k + 1) * 128],
                                                ln1T[k][:, s * SW:(s + 1) * SW],
                                                start=(k == 0), stop=(k == KC - 1), skip_group_check=True)
                    pe_q.append(cPE.inc(inst))
            for s in range(S):
                pe_all_rope.append(pe_q[m * S + s])
                if ridx == 0:
                    DmaQ.wait_toks(nc.vector, rope_toks_q)
                    DmaQ.wait_toks(nc.gpsimd, rope_toks_q)
                emit_rope(ridx, p_q[s], qT[m], slice(s * SW, (s + 1) * SW), r_qc, r_qs)
                ridx += 1
            if m + 2 < h:
                emit_wq_slab(m + 2)
            elif m + 2 == h:
                dst = wkvb.ap().rearrange("p (k c) -> p k c", c=2 * DH)
                wkv_tok[0] = qW.dma(dst, Wkv_r)
        wtok(nc.tensor, wkv_tok[0])
        DmaQ.wait_toks(nc.vector, rope_toks_k)
        DmaQ.wait_toks(nc.gpsimd, rope_toks_k)
        pe_k = []
        for s in range(S):
            nc.tensor.wait_ge(cSC.h, sc_rope[(h - 1) * S + s])
            inst = None
            for k in range(KC):
                inst = nc.tensor.matmul(p_q[s][:], wkvb[:, k * 2 * DH:k * 2 * DH + DH],
                                        ln1T[k][:, s * SW:(s + 1) * SW],
                                        start=(k == 0), stop=(k == KC - 1), skip_group_check=True)
            pe_k.append(cPE.inc(inst))
            pe_all_rope.append(pe_k[s])
            emit_rope(ridx, p_q[s], kTt, slice(s * SW, (s + 1) * SW), r_kc, r_ks)
            ridx += 1
        dv_ropek_last = dv_rope[-1]
        sc_v = []
        for tc in range(TC):
            if tc >= 2:
                nc.tensor.wait_ge(cSC.h, sc_v[tc - 2])
            inst = None
            for k in range(KC):
                inst = nc.tensor.matmul(p_v[tc % 2][:], ln1T[k][:, tc * 128:(tc + 1) * 128],
                                        wkvb[:, k * 2 * DH + DH:(k + 1) * 2 * DH],
                                        start=(k == 0), stop=(k == KC - 1), skip_group_check=True)
            pv = cPE.inc(inst)
            nc.scalar.wait_ge(cPE.h, pv)
            sc_v.append(cSC.inc(nc.scalar.activation(vN[:, tc * DH:(tc + 1) * DH], p_v[tc % 2][:], AF.Copy)))
        sc_v_done = sc_v[-1]
        pB_es.close()
        ar_rope.free()
        ar_ln1.free()
        ar_b.free()

        ar_att = nc.auto_arena()
        expT = [ar_att.sbuf_tensor(f"expT{i}", [128, SW], BF16) for i in range(6)]
        rbc = [ar_att.sbuf_tensor(f"rbcC{i}", [128, SW], F32) for i in range(2)]
        rrow = ar_att.sbuf_tensor("rrow", [1, SW], F32)
        rrhi = ar_att.sbuf_tensor("rrhi", [1, SW], BF16)
        xD = [ar_att.sbuf_tensor(f"xD{i}", [128, t], F32) for i in range(2)]
        wob = [ar_att.sbuf_tensor(f"wob{i}", [128, d], BF16) for i in range(2)]
        ar_att.allocate(side="left")
        ar_ao = nc.auto_arena()
        aoT = [ar_ao.sbuf_tensor(f"aoT{m}", [128, t], BF16) for m in range(h)]
        ar_ao.allocate(side="left")
        pC_es = ExitStack()
        p_sim = [pC_es.enter_context(nc.psum_tensor(f"psim{i}", [128, SW], F32)) for i in range(3)]
        p_o = [pC_es.enter_context(nc.psum_tensor(f"po{i}", [128, SW], F32)) for i in range(3)]
        p_su = [pC_es.enter_context(nc.psum_tensor(f"psu{i}", [128, SW], F32)) for i in range(2)]
        c_allones = c_mask[:, 128:256]

        nc.tensor.wait_ge(cDV.h, dv_ropek_last)
        nc.tensor.wait_ge(cSC.h, sc_v_done)

        sim_done, exp_sc, ready, av_done = [], [], [], []
        norm_done, rbc_done, pe_b_l = [], [], []
        pending_tail = [None]
        g = 0
        hs = 0

        rec_l = []

        def emit_tail_sc(bs):
            nc.scalar.wait_ge(cPE.h, last_av_of_block[bs])
            if bs >= 1:
                nc.scalar.wait_ge(cPE.h, pe_b_l[bs - 1])
            v_ln = cSC.inc(nc.scalar.activation(rrow[0:1, :], p_su[bs % 2][0:1, :], AF.Ln))
            nc.scalar.wait_ge(cSC.h, v_ln)
            rec_l.append(cSC.inc(nc.scalar.activation(rrhi[0:1, :], rrow[0:1, :], AF.Exp, scale=K_M1)))

        def emit_tail(bs, bh, bsl):
            nc.tensor.wait_ge(cSC.h, rec_l[bs])
            pe_b = cPE.inc(nc.tensor.matmul(p_su[bs % 2][:], c_onesb1[:], rrhi[0:1, :],
                                            start=True, stop=True, skip_group_check=True))
            pe_b_l.append(pe_b)
            nc.vector.wait_ge(cPE.h, pe_b)
            if bs >= 2:
                nc.vector.wait_ge(cDV.h, norm_done[bs - 2])
            rbc_done.append(cDV.inc(nc.vector.tensor_copy(rbc[bs % 2][:], p_su[bs % 2][:])))
            nc.vector.wait_ge(cDV.h, rbc_done[bs])
            norm_done.append(cDV.inc(nc.vector.tensor_mul(aoT[bh][:, bsl], p_o[bs % 3][:], rbc[bs % 2][:])))

        pend = []
        last_av_of_block = {}

        def emit_av(bs, jc, gg, first, lastc, co):
            if first and bs >= 3:
                nc.tensor.wait_ge(cDV.h, norm_done[bs - 3])
            if first and bs >= 2:
                nc.tensor.wait_ge(cDV.h, rbc_done[bs - 2])
            r = ready[gg]
            nc.tensor.wait_ge(r[1].h, r[0])
            nc.tensor.matmul(p_o[bs % 3][:, co:SW], vN[:, jc * DH:(jc + 1) * DH],
                             expT[gg % 6][:, 0:SW - co],
                             start=first, stop=lastc, skip_group_check=True)
            av_done.append(cPE.inc(nc.tensor.matmul(
                p_su[bs % 2][:, co:SW], c_allones, expT[gg % 6][:, 0:SW - co],
                start=first, stop=lastc, skip_group_check=True)))
            if lastc:
                last_av_of_block[bs] = cPE.n

        DmaQ.wait_toks(nc.vector, consts_toks)
        for hh in range(h):
            for s in range(S):
                jmax = 4 * s + 4
                sl = slice(s * SW, (s + 1) * SW)
                for jc in range(jmax):
                    diag = jc >= 4 * s
                    co = (jc - 4 * s) * 128 if diag else 0
                    w = SW - co
                    if g >= 6:
                        nc.scalar.wait_ge(cPE.h, av_done[g - 6])
                    if g >= 3:
                        nc.tensor.wait_ge(cSC.h, exp_sc[g - 3])
                    sim_done.append(cPE.inc(nc.tensor.matmul(
                        p_sim[g % 3][:, 0:w], kTt[:, jc * 128:(jc + 1) * 128],
                        qT[hh][:, s * SW + co:(s + 1) * SW],
                        start=True, stop=True, skip_group_check=True)))
                    nc.scalar.wait_ge(cPE.h, sim_done[g])
                    exp_sc.append(cSC.inc(nc.scalar.activation(
                        expT[g % 6][:, 0:w], p_sim[g % 3][:, 0:w], AF.Exp, bias=K_SHIFT_P)))
                    if diag:
                        p = jc - 4 * s
                        nc.vector.wait_ge(cSC.h, exp_sc[g])
                        mk = cDV.inc(nc.vector.tensor_mul(
                            expT[g % 6][:, 0:128], expT[g % 6][:, 0:128],
                            c_mask[:, p * SW + co:p * SW + co + 128]))
                        ready.append((mk, cDV))
                    else:
                        ready.append((exp_sc[g], cSC))
                    pend.append((hs, jc, g, jc == 0, jc == jmax - 1, co))
                    while len(pend) > 2:
                        emit_av(*pend.pop(0))
                    if jc == 1 and pending_tail[0] is not None:
                        emit_tail_sc(pending_tail[0][0])
                    if jc == 3 and pending_tail[0] is not None:
                        emit_tail(*pending_tail[0])
                        pending_tail[0] = None
                    g += 1
                pending_tail[0] = (hs, hh, sl)
                hs += 1
        while pend:
            emit_av(*pend.pop(0))
        emit_tail_sc(pending_tail[0][0])
        emit_tail(*pending_tail[0])
        c_done_dv = norm_done[-1]
        pC_es.close()
        ar_qkv.free()

        ar_wgu = nc.auto_arena()
        wgb = [ar_wgu.sbuf_tensor(f"wgb{i}", [128, d], BF16) for i in range(3)]
        wub = [ar_wgu.sbuf_tensor(f"wub{i}", [128, d], BF16) for i in range(3)]
        ar_wgu.allocate(side="right")
        ar_x2 = nc.auto_arena()
        x2b = [ar_x2.sbuf_tensor(f"x2b{k}", [128, t], BF16) for k in range(3)]
        x2m = [ar_x2.sbuf_tensor(f"x2m{i}", [128, t], F32) for i in range(2)]
        sqD = [ar_x2.sbuf_tensor(f"Dsq{i}", [128, t], BF16) for i in range(2)]
        ar_x2.allocate(side="right")
        stD_es = ExitStack()
        pD_st = [stD_es.enter_context(nc.psum_tensor(f"Dpst{s}", [128, SW], F32)) for s in range(S)]
        pD_sx = [pD_st[s][0:1, :] for s in range(S)]
        pD_sx2 = [pD_st[s][32:33, :] for s in range(S)]
        pD_es = ExitStack()
        p_d = [pD_es.enter_context(nc.psum_tensor(f"pd{s}", [128, SW], F32)) for s in range(S)]

        Wout_r = Wout.rearrange("(k p) (m c) -> m p k c", p=128, c=128)
        WgT_r = WgT.rearrange("(k p) (m c) -> m p k c", p=128, c=128)
        WuT_r = WuT.rearrange("(k p) (m c) -> m p k c", p=128, c=128)
        wo_ld, pe_d, dv_add, st_x2 = [], [], [], []
        d_sc, d_pe = [], []
        wg_ld, wu_ld = [], []

        def emit_stats_D(m):
            nc.tensor.wait_ge(cSC.h, d_sc[m])
            last = None
            for s in range(S):
                sl = slice(s * SW, (s + 1) * SW)
                nc.tensor.matmul(pD_sx[s], c_ones[:], x2b[m % 3][:, sl],
                                 start=(m == 0), stop=(m == KC - 1), skip_group_check=True)
                last = nc.tensor.matmul(pD_sx2[s], c_ones[:], sqD[m % 2][:, sl],
                                        start=(m == 0), stop=(m == KC - 1), skip_group_check=True)
            d_pe.append(cPE.inc(last))

        v_ad_tok = []
        pre_j = 0
        for m in range(KC):
            if m >= 2:
                nc.gpsimd.wait_ge(cPE.h, pe_d[m - 2][S - 1])
            if m == 0:
                nc.gpsimd.wait_ge(cDV.h, c_done_dv)
            dst = wob[m % 2].ap().rearrange("p (k c) -> p k c", c=128)
            wo_ld.append(qW.dma(dst, Wout_r[m]))
            if ((m % 2 == 1 and m <= 13) or m == 2) and pre_j < 6:
                hcp, up_half = pre_j // 2, pre_j % 2
                pre_j += 1
                if up_half == 0:
                    dstp = wgb[hcp].ap().rearrange("p (k c) -> p k c", c=128)
                    wg_ld.append(qW.dma(dstp, WgT_r[hcp]))
                else:
                    dstp = wub[hcp].ap().rearrange("p (k c) -> p k c", c=128)
                    wu_ld.append(qW.dma(dstp, WuT_r[hcp]))
            if m == 0:
                nc.sync.wait_ge(cSC.h, sc_v_done)
                nc.sync.wait_ge(cDV.h, dv_ropek_last)
            if m >= 2:
                nc.sync.wait_ge(cDV.h, dv_add[m - 2])
            ld_m = qL.dma(xD[m % 2][:], xT[m * 128:(m + 1) * 128, :])
            wtok(nc.tensor, wo_ld[m])
            if m == 0:
                nc.tensor.wait_ge(cDV.h, c_done_dv)
            grp = []
            toks = []
            for s in range(S):
                sl = slice(s * SW, (s + 1) * SW)
                if m >= 1:
                    nc.tensor.wait_ge(cDV.h, v_ad_tok[m - 1][s])
                inst = None
                for k in range(KC):
                    inst = nc.tensor.matmul(p_d[s][:], wob[m % 2][:, k * 128:(k + 1) * 128],
                                            aoT[k][:, sl],
                                            start=(k == 0), stop=(k == KC - 1), skip_group_check=True)
                grp.append(cPE.inc(inst))
                if s == 0:
                    wtok(nc.vector, ld_m)
                    if m >= 2:
                        wtok(nc.vector, st_x2[m - 2])
                nc.vector.wait_ge(cPE.h, grp[s])
                toks.append(cDV.inc(nc.vector.tensor_add(
                    x2m[m % 2][:, sl], p_d[s][:], xD[m % 2][:, sl])))
            pe_d.append(grp)
            v_ad_tok.append(toks)
            if m >= 1:
                emit_stats_D(m - 1)
            nc.vector.wait_ge(cDV.h, toks[S - 1])
            if m >= 3:
                nc.vector.wait_ge(cPE.h, d_pe[m - 3])
            dv_add.append(cDV.inc(nc.vector.tensor_copy(x2b[m % 3][:], x2m[m % 2][:])))
            nc.scalar.wait_ge(cDV.h, dv_add[m])
            st_x2.append(qS.dma(x2_scr[m * 128:(m + 1) * 128, :], x2m[m % 2][:]))
            if m >= 2:
                nc.scalar.wait_ge(cPE.h, d_pe[m - 2])
            d_sc.append(cSC.inc(nc.scalar.activation(sqD[m % 2][:], x2b[m % 3][:], AF.Square)))
        emit_stats_D(KC - 1)
        pD_es.close()
        ar_ao.free()

        ar_dst = nc.auto_arena()
        meanD = ar_dst.sbuf_tensor("Dmbc", [128, t], BF16)
        rstdD = ar_dst.sbuf_tensor("Drbc", [128, t], BF16)
        tmpD = ar_dst.sbuf_tensor("Dtmp", [128, t], BF16)
        rowD = [ar_dst.sbuf_tensor(f"DrowF{i}", [1, t], F32) for i in range(3)]
        ar_dst.allocate(side="left")
        rwD = ([rowD[0][0:1, :], rowD[1][0:1, :], rowD[2][0:1, :]]
               + [sqD[0][0:1, :], sqD[1][0:1, :]])
        dv_bcD, pe_bcD = ln_finalize_bcast(rwD, meanD, rstdD, pD_sx, pD_sx2, d_pe[-1], stD_es, "D")
        ar_x2.free()
        ar_ln2 = nc.auto_arena()
        ln2T = [ar_ln2.sbuf_tensor(f"ln2T{k}", [128, t], BF16) for k in range(KC)]
        ar_ln2.allocate(side="right")
        d_ap = []
        DmaQ.wait_toks(nc.vector, st_x2)
        for m in range(KC):
            if m < 2:
                nc.sync.wait_ge(cDV.h, dv_add[KC - 1])
            if m >= 2:
                nc.sync.wait_ge(cDV.h, d_ap[m - 2])
            wtok(nc.sync, st_x2[m])
            t_ap2 = qL.dma(xD[m % 2][:], x2_scr[m * 128:(m + 1) * 128, :])
            wtok(nc.vector, t_ap2)
            if m == 0:
                nc.vector.wait_ge(cDV.h, dv_bcD)
            if m >= 1:
                nc.vector.wait_ge(cDV.h, d_ap[m - 1])
            v_sub = cDV.inc(nc.vector.tensor_sub(tmpD[:], xD[m % 2][:], meanD[:]))
            nc.vector.wait_ge(cDV.h, v_sub)
            d_ap.append(cDV.inc(nc.vector.tensor_mul(ln2T[m][:], tmpD[:], rstdD[:])))
        ln2_done = d_ap[-1]
        ar_dst.free()
        ar_att.free()

        ar_fpre = nc.auto_arena()
        wdb = [ar_fpre.sbuf_tensor(f"wdb{i}", [128, d], BF16) for i in range(WDR)]
        gtb = [ar_fpre.sbuf_tensor(f"gtb{i}", [128, FSLAB * TH], BF16) for i in range(2)]
        ar_fpre.allocate(side="left")
        ar_e = nc.auto_arena()
        sgt = ar_e.sbuf_tensor("sgt", [128, t], F32)
        gt = [ar_e.sbuf_tensor(f"gt{i}", [128, t], BF16) for i in range(2)]
        ar_e.allocate(side="right")
        pE_es = ExitStack()
        p_g = pE_es.enter_context(nc.psum_tensor("pg", [128, t], F32))
        p_u = pE_es.enter_context(nc.psum_tensor("pu", [128, t], F32))

        pe_g, pe_u = [], []
        sc_silu, dv_mult, st_g = [], [], []
        wd_ld = []
        gwd = 0
        for hc in range(HC):
            if hc >= 3:
                nc.gpsimd.wait_ge(cPE.h, pe_g[hc - 3])
                dst = wgb[hc % 3].ap().rearrange("p (k c) -> p k c", c=128)
                wg_ld.append(qW.dma(dst, WgT_r[hc]))
                nc.gpsimd.wait_ge(cPE.h, pe_u[hc - 3])
                dst = wub[hc % 3].ap().rearrange("p (k c) -> p k c", c=128)
                wu_ld.append(qW.dma(dst, WuT_r[hc]))
            if 16 <= hc < 48 and hc % 2 == 0:
                if gwd == 0:
                    nc.gpsimd.wait_ge(cDV.h, ln2_done)
                wd_ld.append(qW.dma(wdb[gwd % WDR][:], WdT[gwd * 128:(gwd + 1) * 128, :]))
                gwd += 1
            wtok(nc.tensor, wg_ld[hc])
            if hc == 0:
                wtok(nc.tensor, wu_ld[0])
                for k in range(KC):
                    nc.tensor.wait_ge(cDV.h, d_ap[k])
                    for pp, buf in ((p_g, wgb), (p_u, wub)):
                        for s in range(S):
                            sl = slice(s * SW, (s + 1) * SW)
                            inst = nc.tensor.matmul(pp[:, sl], buf[0][:, k * 128:(k + 1) * 128],
                                                    ln2T[k][:, sl],
                                                    start=(k == 0), stop=(k == KC - 1),
                                                    skip_group_check=True)
                            if k == KC - 1:
                                if pp is p_g and s == S - 1:
                                    pe_g.append(cPE.inc(inst))
                                elif pp is p_u and s == S - 1:
                                    pe_u.append(cPE.inc(inst))
            else:
                nc.tensor.wait_ge(cSC.h, sc_silu[hc - 1])
                inst = None
                for k in range(KC):
                    for s in range(S):
                        sl = slice(s * SW, (s + 1) * SW)
                        inst = nc.tensor.matmul(p_g[:, sl], wgb[hc % 3][:, k * 128:(k + 1) * 128],
                                                ln2T[k][:, sl],
                                                start=(k == 0), stop=(k == KC - 1), skip_group_check=True)
                pe_g.append(cPE.inc(inst))
                wtok(nc.tensor, wu_ld[hc])
                nc.tensor.wait_ge(cDV.h, dv_mult[hc - 1])
                for k in range(KC):
                    for s in range(S):
                        sl = slice(s * SW, (s + 1) * SW)
                        inst = nc.tensor.matmul(p_u[:, sl], wub[hc % 3][:, k * 128:(k + 1) * 128],
                                                ln2T[k][:, sl],
                                                start=(k == 0), stop=(k == KC - 1), skip_group_check=True)
                pe_u.append(cPE.inc(inst))
            nc.scalar.wait_ge(cPE.h, pe_g[hc])
            if hc >= 1:
                nc.scalar.wait_ge(cDV.h, dv_mult[hc - 1])
            sc_silu.append(cSC.inc(nc.scalar.activation(sgt[:], p_g[:], AF.Silu)))
            nc.vector.wait_ge(cSC.h, sc_silu[hc])
            nc.vector.wait_ge(cPE.h, pe_u[hc])
            if hc >= 2:
                wtok(nc.vector, st_g[hc - 2])
            dv_mult.append(cDV.inc(nc.vector.tensor_mul(gt[hc % 2][:], sgt[:], p_u[:])))
            nc.scalar.wait_ge(cDV.h, dv_mult[hc])
            st_g.append(qS.dma(g_scr[hc * 128:(hc + 1) * 128, :], gt[hc % 2][:]))
        pE_es.close()
        ar_e.free()
        ar_ln2.free()
        ar_wgu.free()

        ar_acc = nc.auto_arena()
        acc = [ar_acc.sbuf_tensor(f"acc{m}", [128, TH], F32) for m in range(KC)]
        x2r = [ar_acc.sbuf_tensor(f"x2r{i}", [128, TH], F32) for i in range(4)]
        ar_acc.allocate(side="left")
        pF_es = ExitStack()
        p_f = [pF_es.enter_context(nc.psum_tensor(f"pf{i}", [128, TH], F32)) for i in range(2)]

        g_scr_r = g_scr.rearrange("(n p) tt -> p n tt", p=128)

        order = ([(0, s) for s in range(NSLAB)]
                 + [(1, NSLAB - 2), (1, NSLAB - 1)]
                 + [(1, s) for s in range(NSLAB - 3, -1, -1)])
        gt_ld, slab_pe = [], []
        dv_f_hist, st_out = [], []
        TH_S = TH // SW
        wd_toks_for = {0: wd_ld[0:FSLAB], 1: wd_ld[FSLAB:2 * FSLAB]}
        wd_store_toks = {}
        for gs, (hf, slab) in enumerate(order):
            DmaQ.wait_toks(nc.sync, st_g[:(slab + 1) * FSLAB])
            if gs >= 2:
                nc.sync.wait_ge(cPE.h, slab_pe[gs - 2])
            dst = gtb[gs % 2].ap().rearrange("p (n tt) -> p n tt", tt=TH)
            src = g_scr_r[:, slab * FSLAB:(slab + 1) * FSLAB, hf * TH:(hf + 1) * TH]
            gt_ld.append(qL.dma(dst, src))
            nx = gs + 1
            if nx < len(order) and nx >= 2 and nx not in (8, 9):
                toks = []
                for hcl in range(FSLAB):
                    hc = order[nx][1] * FSLAB + hcl
                    if order[nx][0] == 0:
                        if gs >= 1:
                            nc.gpsimd.wait_ge(cPE.h, slab_pe[gs - 1])
                        if hcl == 0 and (nx - 2) in wd_store_toks:
                            DmaQ.wait_toks(nc.gpsimd, wd_store_toks[nx - 2])
                        toks.append(qW.dma(wdb[(nx % 2) * FSLAB + hcl][:],
                                           WdT[hc * 128:(hc + 1) * 128, :]))
                    else:
                        nc.scalar.wait_ge(cPE.h, slab_pe[gs - 1])
                        if hcl == 0:
                            DmaQ.wait_toks(nc.scalar, wd_store_toks[order[nx][1]])
                        toks.append(qS.dma(wdb[(nx % 2) * FSLAB + hcl][:],
                                           wd_scr[hc * 128:(hc + 1) * 128, :]))
                wd_toks_for[nx] = toks
            if gs in wd_toks_for:
                DmaQ.wait_toks(nc.tensor, wd_toks_for[gs])
            if hf == 0 and slab <= 5:
                stoks = []
                for hcl in range(FSLAB):
                    hc = slab * FSLAB + hcl
                    wtok(nc.scalar, wd_toks_for[gs][hcl])
                    stoks.append(qS.dma(wd_scr[hc * 128:(hc + 1) * 128, :],
                                        wdb[(gs % 2) * FSLAB + hcl][:]))
                wd_store_toks[slab] = stoks
            wtok(nc.tensor, gt_ld[gs])
            if gs == 0:
                nc.tensor.wait_ge(cDV.h, dv_mult[-1])
            first_of_half = (gs % NSLAB == 0)
            last_of_half = (gs % NSLAB == NSLAB - 1)
            for m in range(KC):
                if len(dv_f_hist) >= 2:
                    nc.tensor.wait_ge(cDV.h, dv_f_hist[-2])
                inst = None
                for hcl in range(FSLAB):
                    wt = wdb[(gs % 2) * FSLAB + hcl]
                    for s2 in range(TH_S):
                        sl = slice(s2 * SW, (s2 + 1) * SW)
                        inst = nc.tensor.matmul(
                            p_f[m % 2][:, sl], wt[:, m * 128:(m + 1) * 128],
                            gtb[gs % 2][:, hcl * TH + s2 * SW:hcl * TH + (s2 + 1) * SW],
                            start=(hcl == 0), stop=(hcl == FSLAB - 1), skip_group_check=True)
                pe_fm = cPE.inc(inst)
                if last_of_half:
                    qbase = hf * KC

                    def issue_x2r(mm):
                        qq = qbase + mm
                        qx = qL if mm % 2 == 0 else qW
                        if qq >= 4:
                            wtok(qx.eng, st_out[qq - 4])
                        x2r_toks[mm] = qx.dma(
                            x2r[qq % 4][:],
                            x2_scr[mm * 128:(mm + 1) * 128, hf * TH:(hf + 1) * TH])

                    if m == 0:
                        x2r_toks = {}
                        if hf == 0:
                            DmaQ.wait_toks(nc.sync, st_g)
                            DmaQ.wait_toks(nc.gpsimd, st_g)
                        issue_x2r(0)
                        issue_x2r(1)
                    if m + 2 < KC:
                        issue_x2r(m + 2)
                nc.vector.wait_ge(cPE.h, pe_fm)
                if first_of_half:
                    dv_f = cDV.inc(nc.vector.tensor_copy(acc[m][:], p_f[m % 2][:]))
                elif not last_of_half:
                    dv_f = cDV.inc(nc.vector.tensor_add(acc[m][:], acc[m][:], p_f[m % 2][:]))
                else:
                    xsl = x2r[(hf * KC + m) % 4]
                    v_ac = cDV.inc(nc.vector.tensor_add(acc[m][:], acc[m][:], p_f[m % 2][:]))
                    wtok(nc.vector, x2r_toks[m])
                    nc.vector.wait_ge(cDV.h, v_ac)
                    dv_f = cDV.inc(nc.vector.tensor_add(xsl[:], acc[m][:], xsl[:]))
                    if hf == 1 and m >= KC - 2:
                        HT2 = TH // 2
                        qa, qb2 = (qS, qL) if m % 2 == 0 else (qW, qS)
                        for qx, c0 in ((qa, 0), (qb2, HT2)):
                            qx.eng.wait_ge(cDV.h, dv_f)
                            st_out.append(qx.dma(
                                outT[m * 128:(m + 1) * 128, hf * TH + c0:hf * TH + c0 + HT2],
                                xsl[:, c0:c0 + HT2]))
                    else:
                        stq = qS if (hf == 0 or m % 2 == 0) else qW
                        stq.eng.wait_ge(cDV.h, dv_f)
                        st_out.append(stq.dma(outT[m * 128:(m + 1) * 128, hf * TH:(hf + 1) * TH],
                                              xsl[:]))
                dv_f_hist.append(dv_f)
            slab_pe.append(cPE.n)
        pF_es.close()
        ar_acc.free()
        ar_fpre.free()

        qS.wait_all(nc.scalar)
        qL.wait_all(nc.sync)
        qW.wait_all(nc.gpsimd)
    return nc


def _host_consts(t=T, d=D):
    inv_freq = (1.0 / (10000.0 ** (np.arange(0, DH, 2, dtype=np.float32) / np.float32(DH)))).astype(np.float32)
    pos = np.arange(t, dtype=np.float32)
    freqs = pos[None, :] * inv_freq[:, None]
    cos = np.cos(np.concatenate([freqs, freqs], 0)).astype(np.float32)
    sin = np.sin(np.concatenate([freqs, freqs], 0)).astype(np.float32)
    sgn = np.concatenate([-np.ones((64, t), np.float32), np.ones((64, t), np.float32)], 0)
    scale = np.float32(DH ** -0.5)
    ropeQc = (cos * scale).astype(np.float32)
    ropeQs = np.roll((sin * sgn * scale).astype(np.float32), 64, axis=0)
    ropeKc = cos
    ropeKs = np.roll((sin * sgn).astype(np.float32), 64, axis=0)
    m = np.zeros((128, 4 * SW), np.float32)
    for p in range(4):
        jr = np.arange(128)[:, None]
        ic = np.arange(SW)[None, :]
        m[:, p * SW:(p + 1) * SW] = (ic >= p * 128 + jr)
    masks = m.astype(ml_dtypes.bfloat16)
    onesb = np.ones((128, 1), ml_dtypes.bfloat16)
    onesf = np.ones((1, 128), np.float32)
    onesb1 = np.ones((1, 128), ml_dtypes.bfloat16)
    cstv = np.zeros((128, 8), np.float32)
    cstv[:, 0] = np.float32(1.0) / np.float32(d)
    cstv[:, 1] = np.float32(EPS)
    cstv[:, 2] = np.float32(-0.5)
    cstv[:, 3] = np.float32(-1.0)
    cstv[:, 4] = np.float32(-EXP_SHIFT)
    return dict(ropeQc=ropeQc, ropeQs=ropeQs, ropeKc=ropeKc, ropeKs=ropeKs,
                masks=masks, onesb=onesb, onesf=onesf, onesb1=onesb1, cst=cstv)


def kernel(L0_attn_x, L0_attn_Wq, L0_attn_Wkv, L0_attn_Wout,
           L0_ffn_w_gate, L0_ffn_w_up, L0_ffn_w_down):
    x = np.asarray(L0_attn_x, dtype=np.float32)
    B = x.shape[0]
    consts = _host_consts()
    Wq_h = np.ascontiguousarray(np.asarray(L0_attn_Wq, dtype=np.float32))
    Wkv_h = np.ascontiguousarray(np.asarray(L0_attn_Wkv, dtype=np.float32))
    Wout_h = np.ascontiguousarray(np.asarray(L0_attn_Wout, dtype=np.float32))
    WgT_h = np.ascontiguousarray(np.asarray(L0_ffn_w_gate, dtype=np.float32).T)
    WuT_h = np.ascontiguousarray(np.asarray(L0_ffn_w_up, dtype=np.float32).T)
    WdT_h = np.ascontiguousarray(np.asarray(L0_ffn_w_down, dtype=np.float32).T)

    nc = bass.Bass("TRN2", target_bir_lowering=False)
    build(nc)

    in_maps = []
    for c in range(B):
        m = dict(consts)
        m["xT"] = np.ascontiguousarray(x[c].T)
        m["Wq"] = Wq_h
        m["Wkv"] = Wkv_h
        m["Wout"] = Wout_h
        m["WgT"] = WgT_h
        m["WuT"] = WuT_h
        m["WdT"] = WdT_h
        in_maps.append(m)
    res = run_bass_kernel_spmd(nc, in_maps, core_ids=list(range(B)))
    out = np.stack([np.asarray(res.results[c]["outT"]).T for c in range(B)], axis=0)
    return np.ascontiguousarray(out.astype(np.float32))
